# version 02
# baseline (speedup 1.0000x reference)
import numpy as np
import ml_dtypes
from contextlib import ExitStack

import concourse.bass as bass
import concourse.mybir as mybir
from concourse.bass_utils import run_bass_kernel_spmd

F32 = mybir.dt.float32
BF16 = mybir.dt.bfloat16
AF = mybir.ActivationFunctionType
ALU = mybir.AluOpType
AX = mybir.AxisListType

B, T, D = 8, 2048, 1024
NCH = 8
HO, WOW, HD = 16, 64, 64
NL = 6
WIN = 256
DIN = 2730
DP = 2816
NMU = 22
TAU, NTAU = 512, 4
TFF, NTFF = 256, 8
FFG = [6, 6, 5, 5]
NEG = -1.0e30

XS_O = 0
QB_O = 4096
KR_O = 12288
VR_O = 18432
AO_O = 0
SQ_O = 24576
A1_O = 25600
PBS_O = 26624
PT_O = 27392
TABS_O = 28160
WQK_O = 32256
WV_O = 36352
WOS_O = 44544
RNB_O = 46592
LSQ_O = 47616
LXS_O = 48640
POOLB_COLS = 55296
XNB_O = 0
GB_O = 16384
UB_O = 16896
WSA_O = 18432
WSB_O = 36864
QRAW_O = 0
RNQ_O = 4096
RNK_O = 4608
MU_O, S2_O, RSTD_O, BB_O = 0, 512, 1024, 1536
M1_O = 2048
USL_O = 0
FMU_O = 4096
FS2_O = 4608
FM1_O = 5120
POOLF_COLS = 6144


def _host_consts():
    half = np.arange(0, HD, 2, dtype=np.float64)
    inv_freq = 1.0 / 10000.0 ** (half / HD)
    t = np.arange(T, dtype=np.float64)
    freqs = np.concatenate([t[:, None] * inv_freq[None, :]] * 2, -1)
    base_scale = (half + 0.4 * HD) / (1.4 * HD)
    power = (t - T // 2) / (WIN // 2)
    scale = np.concatenate([base_scale[None, :] ** power[:, None]] * 2, -1)
    cos, sin = np.cos(freqs), np.sin(freqs)

    def dup(a):
        return np.tile(np.ascontiguousarray(a.T), (2, 1)).astype(np.float32)
    tabs = np.concatenate([dup(8.0 * cos * scale), dup(8.0 * sin * scale),
                           dup(cos / scale), dup(sin / scale)], axis=1)

    P = np.zeros((HD, HD), dtype=np.float32)
    for d in range(32):
        P[d, d + 32] = -1.0
        P[d + 32, d] = 1.0
    permT = np.zeros((128, 128), dtype=np.float32)
    permT[0:64, 0:64] = P.T
    permT[64:128, 64:128] = P.T

    i_l = np.arange(128)[:, None]
    k_l = np.arange(384)[None, :]
    band = (k_l >= i_l) & (k_l <= i_l + 256)
    m0 = np.where(band & (k_l >= 256), 0.0, NEG)
    m1 = np.where(band & (k_l >= 128), 0.0, NEG)
    mg = np.where(band, 0.0, NEG)
    ident = np.eye(128, dtype=np.float32)
    ihl = np.eye(128, dtype=np.float32); ihl[:, 64:] = 0.0; ihl[64:, :] = 0.0
    ihr = np.eye(128, dtype=np.float32); ihr[:, :64] = 0.0; ihr[:64, :] = 0.0
    indc = np.zeros((8, 128, 128), dtype=np.float32)
    indct = np.zeros((8, 128, 128), dtype=np.float32)
    for c in range(8):
        for j in range(2):
            indc[c, j * 64:(j + 1) * 64, 2 * c + j] = 1.0
            indct[c, 2 * c + j, j * 64:(j + 1) * 64] = 1.0
    onesneg = np.full((128, 128), -1.0 / 1024.0, dtype=np.float32)
    onespos = np.full((128, 128), 1.0 / 1024.0, dtype=np.float32)
    return tabs, permT, (m0, m1, mg), ident, ihl, ihr, indc, indct, onesneg, onespos


def _pack_cols(named, dtype):
    offs, cols = {}, 0
    for nm, a in named:
        offs[nm] = cols
        cols += a.shape[1]
    out = np.zeros((128, cols), dtype=dtype)
    for nm, a in named:
        out[: a.shape[0], offs[nm]: offs[nm] + a.shape[1]] = a
    return out, offs, cols


def prep_shared(inp):
    tabs, permT, masks, ident, ihl, ihr, indc, indct, onesneg, onespos = _host_consts()
    cb_named = [("ident", ident), ("perm", permT), ("ihl", ihl), ("ihr", ihr),
                ("m0", masks[0]), ("m1", masks[1]), ("mg", masks[2]),
                ("onesneg", onesneg), ("onespos", onespos)]
    for c in range(8):
        cb_named.append((f"indc{c}", indc[c]))
    for c in range(8):
        cb_named.append((f"indct{c}", indct[c]))
    CB, cbm, cb_cols = _pack_cols(cb_named, np.float32)

    qkv_w = np.asarray(inp["L0_attn_qkv_w"], np.float32)
    qkv_b = np.asarray(inp["L0_attn_qkv_b"], np.float32)
    proj_w = np.asarray(inp["L0_attn_proj_w"], np.float32)
    proj_b = np.asarray(inp["L0_attn_proj_b"], np.float32)
    rel_bias = np.asarray(inp["L0_attn_rel_bias"], np.float32)

    def qk_tiles(w):
        out = np.empty((128, 16 * 1024), np.float32)
        k = 0
        for fq in range(16):
            for Dc in range(8):
                out[:, k * 128:(k + 1) * 128] = w[Dc * 128:(Dc + 1) * 128,
                                                  fq * 128:(fq + 1) * 128]
                k += 1
        return out

    def v_tiles(w):
        out = np.empty((128, 16 * 512), np.float32)
        k = 0
        for fv in range(2):
            for Dc in range(8):
                out[:, k * 512:(k + 1) * 512] = \
                    w[Dc * 128:(Dc + 1) * 128,
                      2048 + fv * 512: 2048 + (fv + 1) * 512]
                k += 1
        return out

    def o_tiles(w):
        out = np.empty((128, 64 * 128), np.float32)
        k = 0
        for fo in range(8):
            for c in range(8):
                out[:, k * 128:(k + 1) * 128] = w[c * 128:(c + 1) * 128,
                                                  fo * 128:(fo + 1) * 128]
                k += 1
        return out

    shared = {"cb": CB, "tab": tabs,
              "wqkv_o": qk_tiles(qkv_w), "wv_o": v_tiles(qkv_w),
              "wo_o": o_tiles(proj_w)}

    i = np.arange(WOW)[:, None]
    j = np.arange(WOW)[None, :]
    bias = rel_bias[i - j + (WOW - 1)]
    Brow = np.transpose(bias, (0, 2, 1)).reshape(WOW, HO * WOW)
    shared["obias"] = np.concatenate([Brow, Brow], 0).astype(np.float32)

    Wqkv = np.asarray(inp["L0_ffn_Wqkv"], np.float32)
    Wo = np.asarray(inp["L0_ffn_Wo"], np.float32)
    W1 = np.asarray(inp["L0_ffn_W1"], np.float32)
    W2 = np.asarray(inp["L0_ffn_W2"], np.float32)
    for li in range(NL):
        shared[f"wqkv_{li}"] = qk_tiles(Wqkv[li])
        shared[f"wv_{li}"] = v_tiles(Wqkv[li])
        shared[f"wo_{li}"] = o_tiles(Wo[li])
        w1a = np.zeros((D, DP), np.float32); w1a[:, :DIN] = W1[li][:, :DIN]
        w1g = np.zeros((D, DP), np.float32); w1g[:, :DIN] = W1[li][:, DIN:]
        w2p = np.zeros((DP, D), np.float32); w2p[:DIN] = W2[li]
        tl = np.empty((128, NMU * 3072), np.float32)
        k = 0
        for mu in range(NMU):
            for w_ in (w1a, w1g):
                for Dc in range(8):
                    tl[:, k * 128:(k + 1) * 128] = \
                        w_[Dc * 128:(Dc + 1) * 128, mu * 128:(mu + 1) * 128]
                    k += 1
            for fo in range(8):
                tl[:, k * 128:(k + 1) * 128] = \
                    w2p[mu * 128:(mu + 1) * 128, fo * 128:(fo + 1) * 128]
                k += 1
        shared[f"wf_{li}"] = tl

    qs = np.asarray(inp["L0_ffn_q_scale"], np.float32)
    ks = np.asarray(inp["L0_ffn_k_scale"], np.float32)
    g = np.asarray(inp["L0_ffn_ln_g"], np.float32)
    bl = np.asarray(inp["L0_ffn_ln_b"], np.float32)
    b1 = np.asarray(inp["L0_ffn_b1"], np.float32)
    b2 = np.asarray(inp["L0_ffn_b2"], np.float32)
    sm_named = [("qkv_b", qkv_b.reshape(24, 128).T),
                ("proj_b", proj_b.reshape(8, 128).T)]
    for li in range(NL):
        b1a = np.zeros(DP, np.float32); b1a[:DIN] = b1[li][:DIN]
        b1g = np.zeros(DP, np.float32); b1g[:DIN] = b1[li][DIN:]
        sm_named += [(f"qs{li}", np.tile(qs[li], 2)[:, None]),
                     (f"ks{li}", np.tile(ks[li], 2)[:, None]),
                     (f"g{li}", g[li].reshape(8, 128).T),
                     (f"bl{li}", bl[li].reshape(8, 128).T),
                     (f"b1a{li}", b1a.reshape(NMU, 128).T),
                     (f"b1g{li}", b1g.reshape(NMU, 128).T),
                     (f"b2{li}", b2[li].reshape(8, 128).T)]
    SM, smm, sm_cols = _pack_cols(sm_named, np.float32)
    shared["smalls"] = SM
    return shared, {"cb": cbm, "sm": smm,
                    "cb_cols": cb_cols, "sm_cols": sm_cols}


class Buf:
    __slots__ = ("name", "w", "xw", "readers", "rng")
    def __init__(self, name, rng=None):
        self.name, self.w, self.xw, self.readers, self.rng = \
            name, None, [], [], rng


NDMA = {"sync": 6, "gpsimd": 12}


class Emitter:
    def __init__(self):
        self.sems = {}
        self.engsem = {e: f"s_{e}" for e in
                       ("sync", "gpsimd", "tensor", "vector", "scalar")}
        self.dmasems = {q: [f"d_{q}{i}" for i in range(NDMA[q])]
                        for q in ("sync", "gpsimd")}

    def begin(self, cur, engobj):
        self.cur, self.eng = cur, engobj
        self.emitted = {k: 0 for k in list(self.engsem.values())}
        for q in self.dmasems:
            for nm in self.dmasems[q]:
                self.emitted[nm] = 0
        self.dmarr = {q: 0 for q in self.dmasems}
        self.water = {}
        self.bufs = {}
        self.live = {}

    def B(self, name):
        if name not in self.bufs:
            self.bufs[name] = Buf(name)
        return self.bufs[name]

    def zone(self, pool, start, cols, name):
        key = f"{pool}:{name}"
        b = self.bufs.get(key)
        if (b is not None and b.rng == (start, start + cols)
                and b in self.live.get(pool, [])):
            return b
        nb = Buf(key, (start, start + cols))
        end = start + cols
        inh = []
        keep = []
        for ob in self.live.get(pool, []):
            s, e2 = ob.rng
            if s < end and start < e2:
                if ob.w:
                    inh.append(ob.w)
                inh.extend(ob.xw)
                inh.extend(ob.readers)
                if s < start:
                    rb = Buf(ob.name + "~l", (s, start))
                    rb.w, rb.readers = ob.w, list(ob.readers)
                    rb.xw = list(ob.xw)
                    keep.append(rb)
                if e2 > end:
                    rb = Buf(ob.name + "~r", (end, e2))
                    rb.w, rb.readers = ob.w, list(ob.readers)
                    rb.xw = list(ob.xw)
                    keep.append(rb)
            else:
                keep.append(ob)
        nb.readers = inh
        keep.append(nb)
        self.live[pool] = keep
        self.bufs[key] = nb
        return nb

    def _wait(self, s, v):
        if self.water.get(s, 0) >= v:
            return
        self.water[s] = v
        self.eng.wait_ge(self.sems[s], v)

    def op(self, engine, fn, reads=(), writes=(), inc=True, dma=False,
           waw_skip=False):
        need = {}
        def add(tok):
            if tok:
                s, v = tok
                if v > need.get(s, -1):
                    need[s] = v
        for b in reads:
            add(b.w)
            for tk in b.xw:
                add(tk)
        for b in writes:
            if not waw_skip:
                add(b.w)
                for tk in b.xw:
                    add(tk)
                for tk in b.readers:
                    add(tk)
        for s, v in need.items():
            assert v <= self.emitted[s], (
                f"forward ref {s}>={v} emitted={self.emitted[s]} "
                f"reads={[b.name for b in reads]} "
                f"writes={[(b.name, b.w, b.readers) for b in writes]}")
        dsem = None
        if dma:
            dsem = self.dmasems[engine][self.dmarr[engine] % NDMA[engine]]
            self.dmarr[engine] += 1
            if self.emitted[dsem] > 0:
                v = self.emitted[dsem]
                if v > need.get(dsem, -1):
                    need[dsem] = v
        ins = None
        if self.cur == engine:
            for s, v in sorted(need.items()):
                self._wait(s, v)
            ins = fn(self.eng)
        if dma:
            s = dsem
            self.emitted[s] += 16
            if ins is not None:
                ins.then_inc(self.sems[s], 16)
            tok = (s, self.emitted[s])
        elif inc:
            s = self.engsem[engine]
            self.emitted[s] += 1
            if ins is not None:
                ins.then_inc(self.sems[s], 1)
            tok = (s, self.emitted[s])
        else:
            s = self.engsem[engine]
            tok = (s, self.emitted[s] + 1)
        for b in writes:
            if waw_skip:
                if b.w and b.w != tok:
                    b.xw.append(b.w)
                b.w = tok
            else:
                b.w, b.xw, b.readers = tok, [], []
        for b in reads:
            b.readers.append(tok)
        return ins


def build_program(maps, cfg=None):
    cfg = cfg or {}
    n_inner = cfg.get("n_inner", NL)
    do_outer = cfg.get("do_outer", True)
    do_final = cfg.get("do_final", True)
    do_ffn = cfg.get("do_ffn", True)
    do_iattn = cfg.get("do_iattn", True)

    cbm, smm = maps["cb"], maps["sm"]
    nc = bass.Bass("TRN2", target_bir_lowering=False)

    dr = {}
    def drt(nm, shape):
        dr[nm] = nc.dram_tensor(nm, list(shape), F32, kind="ExternalInput").ap()
    drt("cb", (128, maps["cb_cols"]))
    drt("tab", (128, 4 * T))
    drt("obias", (128, HO * WOW))
    drt("smalls", (128, maps["sm_cols"]))
    for sfx in ["o"] + [str(i) for i in range(NL)]:
        drt(f"wqkv_{sfx}", (128, 16 * 1024))
        drt(f"wv_{sfx}", (128, 16 * 512))
        drt(f"wo_{sfx}", (128, 64 * 128))
    for li in range(NL):
        drt(f"wf_{li}", (128, NMU * 3072))
    x_in = nc.dram_tensor("x0", [D, T], F32, kind="ExternalInput").ap()
    y_out = nc.dram_tensor("y", [D, T], F32, kind="ExternalOutput").ap()
    u_dram = nc.dram_tensor("u_scr", [D, T], F32, kind="Internal").ap()

    em = Emitter()
    ctx = ExitStack()
    X = ctx.enter_context(nc.sbuf_tensor("Xs", [128, NCH * T], F32))
    PB = ctx.enter_context(nc.sbuf_tensor("PBs", [128, POOLB_COLS], BF16))
    PF = ctx.enter_context(nc.sbuf_tensor("PFs", [128, POOLF_COLS], F32))
    CBB = ctx.enter_context(nc.sbuf_tensor("CBBs", [128, maps["cb_cols"]], BF16))
    SM = ctx.enter_context(nc.sbuf_tensor("SMs", [128, maps["sm_cols"]], F32))
    OBI = ctx.enter_context(nc.sbuf_tensor("OBIs", [128, HO * WOW], BF16))
    SV = ctx.enter_context(nc.sbuf_tensor("SVs", [128, 64], F32))
    banks = [ctx.enter_context(nc.psum_tensor(f"pb{i}", [128, 512], F32))
             for i in range(8)]
    _allsems = list(em.engsem.values())
    for q in em.dmasems:
        _allsems += em.dmasems[q]
    for nm in _allsems:
        em.sems[nm] = ctx.enter_context(nc.semaphore(nm))

    def program(cur, engobj):
        em.begin(cur, engobj)
        E = em.op
        Bu = em.B
        BK = [Bu(f"bank{i}") for i in range(8)]

        def zb(start, cols, name):
            return em.zone("b", start, cols, name)

        def zf(start, cols, name):
            return em.zone("f", start, cols, name)

        def xbuf(c, tau):
            return Bu(f"X_{c}_{tau}")

        def cb(nm, w):
            return CBB[:, cbm[nm]:cbm[nm] + w]

        def smc(nm, i):
            return SM[:, smm[nm] + i: smm[nm] + i + 1]

        def dma(queue, out_ap, in_ap, reads=(), writes=()):
            return E(queue, lambda e: e.dma_start(out_ap, in_ap),
                     reads=reads, writes=writes, dma=True)

        def MM(out_ap, lhsT, rhs, start, reads, writes, inc, tp=None, waw=False):
            return E("tensor",
                     lambda e: e.matmul(out_ap, lhsT, rhs, start=start,
                                        stop=True, tile_position=tp,
                                        skip_group_check=True),
                     reads=reads, writes=writes, inc=inc, waw_skip=waw)

        bank_rr = [0]
        def nbank(avoid=()):
            for _ in range(9):
                i = bank_rr[0] % 8
                bank_rr[0] += 1
                if i not in avoid:
                    return i
            raise RuntimeError("no bank")

        dma("gpsimd", CBB[:, :], dr["cb"][:, :], writes=[Bu("CBB")])
        dma("gpsimd", OBI[:, :], dr["obias"][:, :], writes=[Bu("OBI")])
        dma("sync", SM[:, :], dr["smalls"][:, :], writes=[Bu("SM")])
        for c in range(NCH):
            dma("sync", X[:, c * T:(c + 1) * T], x_in[c * 128:(c + 1) * 128, :],
                writes=[xbuf(c, t_) for t_ in range(NTAU)])
        allz_b = zb(0, POOLB_COLS, "initB")
        allz_f = zf(0, POOLF_COLS, "initF")
        E("vector", lambda e: e.memset(PB[:, :], 0.0), writes=[allz_b])
        E("vector", lambda e: e.memset(PF[:, :], 0.0), writes=[allz_f])
        E("vector", lambda e: e.memset(SV[:, :], 0.0), writes=[Bu("SV0")])
        for i in range(8):
            E("vector", lambda e, i=i: e.memset(banks[i][:, :], 0.0),
              writes=[BK[i]])
        E("vector", lambda e: e.tensor_scalar(
            SV[:, 32:40], SM[:, smm["qkv_b"]:smm["qkv_b"] + 8], 0.125, None,
            ALU.mult), reads=[Bu("SM"), Bu("SV0")], writes=[Bu("SVq")])
        E("vector", lambda e: e.memset(SV[:, 40:41], 1e-5),
          reads=[Bu("SV0")], writes=[Bu("SVeps")])
        E("vector", lambda e: e.memset(SV[:, 41:42], 0.125),
          reads=[Bu("SV0")], writes=[Bu("SVsc")])

        IDb = cb("ident", 128)
        PERMb = cb("perm", 128)
        MKS = [cb("m0", 384), cb("m1", 384), cb("mg", 384)]

        def attn_phase(li, outer):
            sfx = "o" if outer else str(li)
            wqk, wvn, won = f"wqkv_{sfx}", f"wv_{sfx}", f"wo_{sfx}"

            zxs = [zb(XS_O + c * 512, 512, f"xs{c}") for c in range(NCH)]
            zqb = [zb(QB_O + h * 512, 512, f"qb{h}") for h in range(16)]
            for h in range(16):
                r0 = 64 if h % 2 == 0 else 0
                E("vector", lambda e, h=h, r0=r0: e.memset(
                    PB[r0:r0 + 64, QB_O + h * 512: QB_O + (h + 1) * 512], 0.0),
                  writes=[zqb[h]])
            zkr = [zb(KR_O + c * 768, 768, f"kr{c}") for c in range(NCH)]
            zvr = [zb(VR_O + s * 1024, 1024, f"vr{s}") for s in range(6)]
            zsq = [zb(SQ_O + s * 512, 512, f"sq{s}") for s in range(2)]
            za1 = [zb(A1_O + s * 512, 512, f"a1{s}") for s in range(2)]
            zpb = [zb(PBS_O + s * 384, 384, f"pexp{s}") for s in range(2)]
            zpt = [zb(PT_O + s * 384, 384, f"ptb{s}") for s in range(2)]
            ztab = [zb(TABS_O + s * 2048, 2048, f"tabs{s}") for s in range(2)]
            zwqk = [zb(WQK_O + s * 1024, 1024, f"wqk{s}") for s in range(4)]
            zwv = zb(WV_O, 8192, "wv")
            zwos = [zb(WOS_O + s * 1024, 1024, f"wos{s}") for s in range(2)]
            zrnb = [zb(RNB_O + s * 512, 512, f"rnb{s}") for s in range(2)]
            zao = None

            for tau in range(cfg.get("only_tau", NTAU)):
                t0 = tau * TAU
                zxs = [zb(XS_O + c * 512, 512, f"xs{c}") for c in range(NCH)]
                if not outer:
                    ts = tau % 2
                    for k in range(4):
                        E("gpsimd", lambda e, k=k, ts=ts: e.dma_start(
                            PB[:, TABS_O + ts * 2048 + k * 512:
                               TABS_O + ts * 2048 + (k + 1) * 512],
                            dr["tab"][:, k * T + t0: k * T + t0 + 512]),
                          writes=[ztab[ts]], dma=True, waw_skip=(k != 0))
                for c in range(NCH):
                    dst = PB[:, XS_O + c * 512: XS_O + (c + 1) * 512]
                    if outer or c < 4:
                        E("scalar", lambda e, dst=dst, c=c: e.copy(
                            dst, X[:, c * T + t0: c * T + t0 + 512]),
                          reads=[xbuf(c, tau)], writes=[zxs[c]])
                    else:
                        E("scalar", lambda e, dst=dst, c=c: e.copy(
                            dst[:, 1:512], X[:, c * T + t0: c * T + t0 + 511]),
                          reads=[xbuf(c, tau)], writes=[zxs[c]])
                        if tau == 0:
                            E("vector", lambda e, dst=dst: e.memset(dst[:, 0:1], 0.0),
                              writes=[zxs[c]])
                        else:
                            E("vector", lambda e, dst=dst, c=c: e.tensor_copy(
                                dst[:, 0:1], SV[:, c - 2:c - 1]),
                              reads=[Bu(f"sv{c}")], writes=[zxs[c]])
                if (not outer) and tau < NTAU - 1:
                    for c in range(4, NCH):
                        E("vector", lambda e, c=c: e.tensor_copy(
                            SV[:, c - 2:c - 1],
                            X[:, c * T + t0 + 511: c * T + t0 + 512]),
                          reads=[xbuf(c, tau)], writes=[Bu(f"sv{c}")])

                for side in range(2):
                    pend_norm = []
                    for c in range(NCH):
                        fq = side * 8 + c
                        slot = fq % 4
                        woff = WQK_O + slot * 1024
                        dma("gpsimd", PB[:, woff:woff + 1024],
                            dr[wqk][:, fq * 1024:(fq + 1) * 1024],
                            writes=[zwqk[slot]])
                        bk = nbank(avoid=(6, 7))
                        for Dc in range(NCH):
                            MM(banks[bk][:, 0:512],
                               PB[:, woff + Dc * 128: woff + (Dc + 1) * 128],
                               PB[:, XS_O + Dc * 512: XS_O + Dc * 512 + 512],
                               start=(Dc == 0),
                               reads=[zwqk[slot], zxs[Dc]],
                               writes=[BK[bk]], inc=(Dc == 7), waw=(Dc != 0))
                        if outer:
                            if side == 0:
                                for hh in range(2):
                                    h = 2 * c + hh
                                    r0, r1 = hh * 64, (hh + 1) * 64
                                    dst = PB[r0:r1, QB_O + h * 512:
                                             QB_O + (h + 1) * 512]
                                    E("scalar", lambda e, dst=dst, bk=bk, c=c,
                                      r0=r0, r1=r1:
                                      e.activation(dst, banks[bk][r0:r1, 0:512],
                                                   AF.Identity,
                                                   bias=SV[r0:r1, 32 + c:33 + c],
                                                   scale=SV[r0:r1, 41:42]),
                                      reads=[BK[bk], Bu("SVq"), Bu("SVsc")],
                                      writes=[zqb[h]])
                            else:
                                dst = PB[:, KR_O + c * 768 + 256:
                                         KR_O + c * 768 + 768]
                                E("scalar", lambda e, dst=dst, bk=bk, c=c:
                                  e.activation(dst, banks[bk][:, 0:512], AF.Identity,
                                               bias=smc("qkv_b", 8 + c)),
                                  reads=[BK[bk], Bu("SM")], writes=[zkr[c]])
                        else:
                            qr = PF[:, QRAW_O + c * 512: QRAW_O + (c + 1) * 512]
                            zqr = zf(QRAW_O + c * 512, 512, f"qraw{c}")
                            E("scalar", lambda e, qr=qr, bk=bk: e.copy(
                                qr, banks[bk][:, 0:512]),
                              reads=[BK[bk]], writes=[zqr])
                            sq = PB[:, SQ_O + (c % 2) * 512:
                                    SQ_O + (c % 2) * 512 + 512]
                            E("vector", lambda e, sq=sq, qr=qr: e.tensor_mul(
                                sq, qr, qr),
                              reads=[zqr], writes=[zsq[c % 2]])
                            nb = 6 if side == 0 else 7
                            pend_norm.append((c, nb))
                            if c >= 1:
                                _emit_norm(pend_norm.pop(0), side)
                    if not outer:
                        while pend_norm:
                            _emit_norm(pend_norm.pop(0), side)
                        _emit_rn(li, side)
                        for c in range(NCH):
                            _rot_a(li, side, c, tau)
                            if c >= 1:
                                _rot_b(li, side, c - 1, tau)
                        _rot_b(li, side, 7, tau)

                dma("gpsimd", PB[:, WV_O:WV_O + 8192], dr[wvn][:, 0:8192],
                    writes=[zwv])
                for m in range(4):
                    jg = tau * 4 + m
                    vslot = m if outer else (jg % 6)
                    for fv in range(2):
                        bk = nbank(avoid=(6, 7))
                        for Dc in range(NCH):
                            MM(banks[bk][:, 0:512],
                               PB[:, XS_O + Dc * 512 + m * 128:
                                  XS_O + Dc * 512 + (m + 1) * 128],
                               PB[:, WV_O + (fv * 8 + Dc) * 512:
                                  WV_O + (fv * 8 + Dc + 1) * 512],
                               start=(Dc == 0),
                               reads=[zwv, zxs[Dc]],
                               writes=[BK[bk]], inc=(Dc == 7), waw=(Dc != 0))
                        dst = PB[:, VR_O + vslot * 1024 + fv * 512:
                                 VR_O + vslot * 1024 + (fv + 1) * 512]
                        E("scalar", lambda e, dst=dst, bk=bk: e.copy(
                            dst, banks[bk][:, 0:512]),
                          reads=[BK[bk]], writes=[zvr[vslot]],
                          waw_skip=(fv == 1))

                zao = [zb(AO_O + c * 512, 512, f"ao{c}") for c in range(NCH)]
                if cfg.get("skip_attn"):
                    for c in range(NCH):
                        E("vector", lambda e, c=c: e.memset(
                            PB[:, AO_O + c * 512: AO_O + (c + 1) * 512], 0.0),
                          writes=[zao[c]])
                elif outer:
                    _outer_attn(tau, zqb, zkr, zvr, zpb, zpt, zao)
                else:
                    _inner_attn(li, tau, zqb, zkr, zvr, zpb, zpt, zao)

                for fo in range(NCH):
                    ws = WOS_O + (fo % 2) * 1024
                    dma("gpsimd", PB[:, ws:ws + 1024],
                        dr[won][:, fo * 1024:(fo + 1) * 1024],
                        writes=[zwos[fo % 2]])
                    bk = nbank(avoid=(6, 7))
                    for c in range(NCH):
                        MM(banks[bk][:, 0:512],
                           PB[:, ws + c * 128: ws + (c + 1) * 128],
                           PB[:, AO_O + c * 512: AO_O + (c + 1) * 512],
                           start=(c == 0),
                           reads=[zwos[fo % 2], zao[c]],
                           writes=[BK[bk]], inc=(c == 7), waw=(c != 0))
                    if outer:
                        E("vector", lambda e, bk=bk, fo=fo:
                          e.scalar_tensor_tensor(
                              X[:, fo * T + t0: fo * T + t0 + 512],
                              banks[bk][:, 0:512], smc("proj_b", fo),
                              X[:, fo * T + t0: fo * T + t0 + 512],
                              ALU.add, ALU.add),
                          reads=[BK[bk], Bu("SM"), xbuf(fo, tau)],
                          writes=[xbuf(fo, tau)])
                    else:
                        E("vector", lambda e, bk=bk, fo=fo: e.tensor_add(
                            X[:, fo * T + t0: fo * T + t0 + 512],
                            banks[bk][:, 0:512],
                            X[:, fo * T + t0: fo * T + t0 + 512]),
                          reads=[BK[bk], xbuf(fo, tau)], writes=[xbuf(fo, tau)])

                if (not outer) and tau < NTAU - 1:
                    for c in range(NCH):
                        E("vector", lambda e, c=c: e.tensor_copy(
                            PB[:, KR_O + c * 768: KR_O + c * 768 + 256],
                            PB[:, KR_O + c * 768 + 512: KR_O + c * 768 + 768]),
                          reads=[zkr[c]], writes=[zkr[c]])

        def _emit_norm(cn, side):
            c, nb = cn
            E = em.op
            sq = PB[:, SQ_O + (c % 2) * 512: SQ_O + (c % 2) * 512 + 512]
            MM_ = lambda e: e.matmul(banks[nb][:, 0:512],
                                     CBB[:, cbm[f"indc{c}"]:cbm[f"indc{c}"] + 128],
                                     sq, start=(c == 0), stop=True,
                                     skip_group_check=True)
            em.op("tensor", MM_,
                  reads=[em.zone("b", SQ_O + (c % 2) * 512, 512, f"sq{c % 2}"),
                         em.B("CBB")],
                  writes=[em.B(f"bank{nb}")], inc=True, waw_skip=(c != 0))

        def _emit_rn(li, side):
            E = em.op
            nb = 6 if side == 0 else 7
            rno = RNQ_O if side == 0 else RNK_O
            rn = PF[:, rno:rno + 512]
            zrn = em.zone("f", rno, 512, f"rn{side}")
            E("scalar", lambda e: e.activation(rn[0:16, :],
                                               banks[nb][0:16, 0:512], AF.Sqrt),
              reads=[em.B(f"bank{nb}")], writes=[zrn])
            E("vector", lambda e: e.tensor_scalar(rn[0:16, :], rn[0:16, :],
                                                  1e-12, None, ALU.max),
              reads=[zrn], writes=[zrn])
            E("vector", lambda e: e.reciprocal(rn[0:16, :], rn[0:16, :]),
              reads=[zrn], writes=[zrn])
            rnb = PB[:, RNB_O + side * 512: RNB_O + side * 512 + 512]
            zrnb = em.zone("b", RNB_O + side * 512, 512, f"rnb{side}")
            E("vector", lambda e, rnb=rnb, rn=rn: e.tensor_copy(
                rnb[0:16, :], rn[0:16, :]),
              reads=[zrn], writes=[zrnb])

        _rot_state = {}

        def _rot_a(li, side, c, tau):
            E = em.op
            ts = tau % 2
            zrnb = em.zone("b", RNB_O + side * 512, 512, f"rnb{side}")
            rnb = PB[:, RNB_O + side * 512: RNB_O + side * 512 + 512]
            bb = nbank(avoid=(6, 7))
            MM_ = lambda e: e.matmul(banks[bb][:, 0:512],
                                     CBB[:, cbm[f"indct{c}"]:cbm[f"indct{c}"] + 128],
                                     rnb[:, 0:512], start=True, stop=True,
                                     skip_group_check=True)
            E("tensor", MM_, reads=[zrnb, em.B("CBB")],
              writes=[em.B(f"bank{bb}")], inc=True)
            a1 = PB[:, A1_O + (c % 2) * 512: A1_O + (c % 2) * 512 + 512]
            za1 = em.zone("b", A1_O + (c % 2) * 512, 512, f"a1{c % 2}")
            zqr = em.zone("f", QRAW_O + c * 512, 512, f"qraw{c}")
            qr = PF[:, QRAW_O + c * 512: QRAW_O + (c + 1) * 512]
            scn = (f"qs{li}" if side == 0 else f"ks{li}")
            sc_ap = SM[:, smm[scn]: smm[scn] + 1]
            E("vector", lambda e, a1=a1, qr=qr, bb=bb, sc_ap=sc_ap:
              e.scalar_tensor_tensor(a1, qr, sc_ap, banks[bb][:, 0:512],
                                     ALU.mult, ALU.mult),
              reads=[zqr, em.B("SM"), em.B(f"bank{bb}")], writes=[za1])
            pp = nbank(avoid=(6, 7, bb))
            MM2_ = lambda e: e.matmul(banks[pp][:, 0:512],
                                      CBB[:, cbm["perm"]:cbm["perm"] + 128],
                                      a1, start=True, stop=True,
                                      skip_group_check=True)
            E("tensor", MM2_, reads=[za1, em.B("CBB")],
              writes=[em.B(f"bank{pp}")], inc=True)
            _rot_state[(side, c)] = pp

        def _rot_b(li, side, c, tau):
            E = em.op
            ts = tau % 2
            pp = _rot_state.pop((side, c))
            toff = TABS_O + ts * 2048 + (0 if side == 0 else 2) * 512
            ct = PB[:, toff:toff + 512]
            st = PB[:, toff + 512:toff + 1024]
            ztab = em.zone("b", TABS_O + ts * 2048, 2048, f"tabs{ts}")
            a1 = PB[:, A1_O + (c % 2) * 512: A1_O + (c % 2) * 512 + 512]
            za1 = em.zone("b", A1_O + (c % 2) * 512, 512, f"a1{c % 2}")
            E("vector", lambda e, a1=a1, ct=ct: e.tensor_mul(a1, a1, ct),
              reads=[za1, ztab], writes=[za1])
            t2 = PB[:, SQ_O + (c % 2) * 512: SQ_O + (c % 2) * 512 + 512]
            zt2 = em.zone("b", SQ_O + (c % 2) * 512, 512, f"sq{c % 2}")
            E("vector", lambda e, t2=t2, pp=pp, st=st: e.tensor_mul(
                t2, banks[pp][:, 0:512], st),
              reads=[em.B(f"bank{pp}"), ztab], writes=[zt2])
            if side == 0:
                for hh in range(2):
                    h = 2 * c + hh
                    r0, r1 = hh * 64, (hh + 1) * 64
                    dst = PB[r0:r1, QB_O + h * 512: QB_O + (h + 1) * 512]
                    wz = em.zone("b", QB_O + h * 512, 512, f"qb{h}")
                    E("vector", lambda e, dst=dst, a1=a1, t2=t2, r0=r0, r1=r1:
                      e.tensor_add(dst, a1[r0:r1, :], t2[r0:r1, :]),
                      reads=[za1, zt2], writes=[wz])
            else:
                dst = PB[:, KR_O + c * 768 + 256: KR_O + c * 768 + 768]
                wz = em.zone("b", KR_O + c * 768, 768, f"kr{c}")
                E("vector", lambda e, dst=dst, a1=a1, t2=t2: e.tensor_add(
                    dst, a1, t2),
                  reads=[za1, zt2], writes=[wz])

        def _inner_attn(li, tau, zqb, zkr, zvr, zpb, zpt, zao):
            E = em.op
            Bu = em.B

            def nb2(avoid):
                for _ in range(9):
                    i = bank_rr[0] % 8
                    bank_rr[0] += 1
                    if i not in avoid:
                        return i

            for c in range(NCH):
                po = nb2((6, 7))
                state = {}

                def stage_a(hh, qt, first_o):
                    qtg = tau * 4 + qt
                    h = 2 * c + hh
                    sb_ = nb2((po, 6, 7))
                    mk = MKS[min(qtg, 2)]
                    MM(banks[sb_][:, 0:384], IDb, mk, start=True,
                       reads=[Bu("CBB")], writes=[Bu(f"bank{sb_}")], inc=False)
                    lq = PB[:, QB_O + h * 512 + qt * 128:
                            QB_O + h * 512 + (qt + 1) * 128]
                    kap = PB[:, KR_O + c * 768 + qt * 128:
                             KR_O + c * 768 + qt * 128 + 384]
                    MM(banks[sb_][:, 0:384], lq, kap, start=False,
                       reads=[zqb[h], zkr[c]],
                       writes=[Bu(f"bank{sb_}")], inc=True, waw=True)
                    state[(hh, qt, "sb")] = sb_

                def stage_b(hh, qt):
                    sb_ = state[(hh, qt, "sb")]
                    par = qt % 2
                    sv_i = 8 + par * 2
                    E("vector", lambda e, sb_=sb_, sv_i=sv_i: e.reduce_max(
                        SV[:, sv_i:sv_i + 1], banks[sb_][:, 0:384], AX.X,
                        negate=True),
                      reads=[Bu(f"bank{sb_}")], writes=[Bu(f"nm{par}")])
                    pb_ = PB[:, PBS_O + par * 384: PBS_O + par * 384 + 384]
                    E("scalar", lambda e, sb_=sb_, pb_=pb_, sv_i=sv_i:
                      e.activation(pb_, banks[sb_][:, 0:384], AF.Exp,
                                   bias=SV[:, sv_i:sv_i + 1],
                                   accum_out=SV[:, sv_i + 1:sv_i + 2]),
                      reads=[Bu(f"bank{sb_}"), Bu(f"nm{par}")],
                      writes=[zpb[par], Bu(f"sum{par}")])
                    E("vector", lambda e, sv_i=sv_i: e.reciprocal(
                        SV[:, sv_i + 1:sv_i + 2], SV[:, sv_i + 1:sv_i + 2]),
                      reads=[Bu(f"sum{par}")], writes=[Bu(f"sum{par}")])
                    E("vector", lambda e, pb_=pb_, sv_i=sv_i: e.tensor_scalar(
                        pb_, pb_, SV[:, sv_i + 1:sv_i + 2], None, ALU.mult),
                      reads=[zpb[par], Bu(f"sum{par}")], writes=[zpb[par]])

                def stage_c(hh, qt):
                    par = qt % 2
                    pt_ = nb2((po, state[(hh, qt, "sb")], 6, 7))
                    for kk in range(3):
                        MM(banks[pt_][:, kk * 128:(kk + 1) * 128],
                           PB[:, PBS_O + par * 384 + kk * 128:
                              PBS_O + par * 384 + (kk + 1) * 128],
                           IDb, start=(kk == 0),
                           reads=[zpb[par], Bu("CBB")],
                           writes=[Bu(f"bank{pt_}")], inc=(kk == 2),
                           waw=(kk != 0))
                    ptb = PB[:, PT_O + par * 384: PT_O + par * 384 + 384]
                    E("scalar", lambda e, ptb=ptb, pt_=pt_: e.copy(
                        ptb, banks[pt_][:, 0:384]),
                      reads=[Bu(f"bank{pt_}")], writes=[zpt[par]])

                def stage_d(hh, qt, first_o, last):
                    qtg = tau * 4 + qt
                    par = qt % 2
                    h = 2 * c + hh
                    first_kk = True
                    for kk in range(3):
                        jg = qtg - 2 + kk
                        if jg < 0:
                            continue
                        vslot = jg % 6
                        MM(banks[po][hh * 64:(hh + 1) * 64,
                                     qt * 128:(qt + 1) * 128],
                           PB[:, VR_O + vslot * 1024 + h * 64:
                              VR_O + vslot * 1024 + (h + 1) * 64],
                           PB[:, PT_O + par * 384 + kk * 128:
                              PT_O + par * 384 + (kk + 1) * 128],
                           start=first_kk,
                           reads=[zvr[vslot], zpt[par]],
                           writes=[Bu(f"bank{po}")],
                           inc=(last and kk == 2),
                           waw=(not first_o), tp=(0, hh * 64))
                        first_o = False
                        first_kk = False
                    return first_o

                steps = [(hh, qt) for hh in range(2) for qt in range(4)]
                first_o = True
                for i, (hh, qt) in enumerate(steps):
                    stage_a(hh, qt, first_o)
                    if i > 0:
                        stage_c(*steps[i - 1])
                    stage_b(hh, qt)
                    if i > 0:
                        phh, pqt = steps[i - 1]
                        first_o = stage_d(phh, pqt, first_o, last=False)
                stage_c(*steps[-1])
                first_o = stage_d(*steps[-1], first_o, True)
                dst = PB[:, AO_O + c * 512: AO_O + (c + 1) * 512]
                E("scalar", lambda e, dst=dst, po=po: e.copy(
                    dst, banks[po][:, 0:512]),
                  reads=[Bu(f"bank{po}")], writes=[zao[c]])

        def _outer_attn(tau, zqb, zkr, zvr, zpb, zpt, zao):
            E = em.op
            Bu = em.B

            def nb2(avoid):
                for _ in range(9):
                    i = bank_rr[0] % 8
                    bank_rr[0] += 1
                    if i not in avoid:
                        return i

            for c in range(cfg.get("only_nc", NCH)):
                po = nb2((6, 7))
                state = {}

                def stage_a(hh, wp):
                    h = 2 * c + hh
                    sb_ = nb2((po, 6, 7))
                    MM(banks[sb_][:, 0:64], IDb, OBI[:, h * 64:(h + 1) * 64],
                       start=True, reads=[Bu("OBI")],
                       writes=[Bu(f"bank{sb_}")], inc=False)
                    for wr in range(2):
                        qo = QB_O + h * 512 + wp * 128 + wr * 64
                        ko = KR_O + c * 768 + 256 + wp * 128 + wr * 64
                        MM(banks[sb_][wr * 64:(wr + 1) * 64, 0:64],
                           PB[:, qo:qo + 64],
                           PB[:, ko:ko + 64],
                           start=False,
                           reads=[zqb[h], zkr[c]],
                           writes=[Bu(f"bank{sb_}")], inc=(wr == 1), waw=True,
                           tp=(0, wr * 64))
                    state[(hh, wp, "sb")] = sb_

                def stage_b(hh, wp):
                    sb_ = state[(hh, wp, "sb")]
                    par = wp % 2
                    sv_i = 8 + par * 2
                    E("vector", lambda e, sb_=sb_, sv_i=sv_i: e.reduce_max(
                        SV[:, sv_i:sv_i + 1], banks[sb_][:, 0:64], AX.X,
                        negate=True),
                      reads=[Bu(f"bank{sb_}")], writes=[Bu(f"nm{par}")])
                    pb_ = PB[:, PBS_O + par * 384: PBS_O + par * 384 + 64]
                    E("scalar", lambda e, sb_=sb_, pb_=pb_, sv_i=sv_i:
                      e.activation(pb_, banks[sb_][:, 0:64], AF.Exp,
                                   bias=SV[:, sv_i:sv_i + 1],
                                   accum_out=SV[:, sv_i + 1:sv_i + 2]),
                      reads=[Bu(f"bank{sb_}"), Bu(f"nm{par}")],
                      writes=[zpb[par], Bu(f"sum{par}")])
                    E("vector", lambda e, sv_i=sv_i: e.reciprocal(
                        SV[:, sv_i + 1:sv_i + 2], SV[:, sv_i + 1:sv_i + 2]),
                      reads=[Bu(f"sum{par}")], writes=[Bu(f"sum{par}")])
                    E("vector", lambda e, pb_=pb_, sv_i=sv_i: e.tensor_scalar(
                        pb_, pb_, SV[:, sv_i + 1:sv_i + 2], None, ALU.mult),
                      reads=[zpb[par], Bu(f"sum{par}")], writes=[zpb[par]])

                def stage_c(hh, wp):
                    par = wp % 2
                    pt_ = nb2((po, state[(hh, wp, "sb")], 6, 7))
                    for wr in range(2):
                        MM(banks[pt_][wr * 64:(wr + 1) * 64, 0:128],
                           PB[:, PBS_O + par * 384: PBS_O + par * 384 + 64],
                           cb("ihl" if wr == 0 else "ihr", 128),
                           start=True,
                           reads=[zpb[par], Bu("CBB")],
                           writes=[Bu(f"bank{pt_}")], inc=(wr == 1),
                           waw=(wr != 0), tp=(0, wr * 64))
                    ptb = PB[:, PT_O + par * 384: PT_O + par * 384 + 128]
                    E("scalar", lambda e, ptb=ptb, pt_=pt_: e.copy(
                        ptb, banks[pt_][:, 0:128]),
                      reads=[Bu(f"bank{pt_}")], writes=[zpt[par]])

                def stage_d(hh, wp, first_o, last):
                    par = wp % 2
                    h = 2 * c + hh
                    MM(banks[po][hh * 64:(hh + 1) * 64, wp * 128:(wp + 1) * 128],
                       PB[:, VR_O + wp * 1024 + h * 64:
                          VR_O + wp * 1024 + (h + 1) * 64],
                       PB[:, PT_O + par * 384: PT_O + par * 384 + 128],
                       start=True,
                       reads=[zvr[wp], zpt[par]],
                       writes=[Bu(f"bank{po}")],
                       inc=last,
                       waw=(not first_o), tp=(0, hh * 64))
                    return False

                steps = [(hh, wp) for hh in range(2) for wp in range(4)]
                first_o = True
                ms = cfg.get("oa_maxstage", 4)
                if ms < 4:
                    for i, (hh, wp) in enumerate(steps):
                        stage_a(hh, wp)
                        if ms >= 2:
                            stage_b(hh, wp)
                        if ms >= 3:
                            stage_c(hh, wp)
                elif cfg.get("nopipe"):
                    for i, (hh, wp) in enumerate(steps):
                        stage_a(hh, wp)
                        stage_b(hh, wp)
                        stage_c(hh, wp)
                        first_o = stage_d(hh, wp, first_o, i == len(steps) - 1)
                else:
                    for i, (hh, wp) in enumerate(steps):
                        stage_a(hh, wp)
                        if i > 0:
                            stage_c(*steps[i - 1])
                        stage_b(hh, wp)
                        if i > 0:
                            first_o = stage_d(*steps[i - 1], first_o, False)
                    stage_c(*steps[-1])
                    first_o = stage_d(*steps[-1], first_o, True)
                dst = PB[:, AO_O + c * 512: AO_O + (c + 1) * 512]
                E("scalar", lambda e, dst=dst, po=po, c=c: e.activation(
                    dst, banks[po][:, 0:512], AF.Identity,
                    bias=smc("qkv_b", 16 + c)),
                  reads=[Bu(f"bank{po}"), Bu("SM")], writes=[zao[c]])

        def layernorm(kind, li=None, store_u=False):
            E = em.op
            Bu = em.B
            for tau in range(NTAU):
                t0 = tau * TAU
                mb, sb2 = 6, 7
                zlx = [em.zone("b", LXS_O + s * 512, 512, f"lxs{s}")
                       for s in range(2)]
                zlq = [em.zone("b", LSQ_O + s * 512, 512, f"lsq{s}")
                       for s in range(2)]
                zmu = em.zone("f", MU_O, 512, "muv")
                zs2 = em.zone("f", S2_O, 512, "s2v")
                zrs = em.zone("f", RSTD_O, 512, "rstd")
                zbv = em.zone("f", BB_O, 512, "Bv")
                zm1 = [em.zone("f", M1_O + s * 512, 512, f"m1{s}")
                       for s in range(2)]

                def build_xcast(c, into_z):
                    xc = PB[:, LXS_O + (c % 2) * 512: LXS_O + (c % 2) * 512 + 512]
                    if kind == "ffn" and c >= 4:
                        E("scalar", lambda e, xc=xc, c=c: e.copy(
                            xc[:, 1:512], X[:, c * T + t0: c * T + t0 + 511]),
                          reads=[xbuf(c, tau)], writes=[into_z])
                        if tau == 0:
                            E("vector", lambda e, xc=xc: e.memset(xc[:, 0:1], 0.0),
                              writes=[into_z])
                        else:
                            E("vector", lambda e, xc=xc, c=c: e.tensor_copy(
                                xc[:, 0:1], X[:, c * T + t0 - 1: c * T + t0]),
                              reads=[xbuf(c, tau - 1)], writes=[into_z])
                    elif kind == "final":
                        m1 = PF[:, FM1_O + (c % 2) * 512: FM1_O + (c % 2) * 512 + 512]
                        zfm = em.zone("f", FM1_O + (c % 2) * 512, 512, f"fm1{c % 2}")
                        E("vector", lambda e, m1=m1, c=c: e.tensor_add(
                            m1, X[:, c * T + t0: c * T + t0 + 512],
                            PF[:, USL_O + c * 512: USL_O + (c + 1) * 512]),
                          reads=[xbuf(c, tau), em.zone("f", USL_O + c * 512, 512,
                                                       f"usl{c}")],
                          writes=[zfm])
                        E("scalar", lambda e, xc=xc, m1=m1: e.copy(xc, m1),
                          reads=[zfm], writes=[into_z])
                    else:
                        E("scalar", lambda e, xc=xc, c=c: e.copy(
                            xc, X[:, c * T + t0: c * T + t0 + 512]),
                          reads=[xbuf(c, tau)], writes=[into_z])
                    return xc

                for c in range(NCH):
                    xc = build_xcast(c, zlx[c % 2])
                    MM(banks[mb][:, 0:512], cb("onesneg", 128), xc,
                       start=(c == 0), reads=[zlx[c % 2], Bu("CBB")],
                       writes=[BK[mb]], inc=True, waw=(c != 0))
                    sq = PB[:, LSQ_O + (c % 2) * 512: LSQ_O + (c % 2) * 512 + 512]
                    E("vector", lambda e, sq=sq, xc=xc: e.tensor_mul(sq, xc, xc),
                      reads=[zlx[c % 2]], writes=[zlq[c % 2]])
                    MM(banks[sb2][:, 0:512], cb("onespos", 128), sq,
                       start=(c == 0), reads=[zlq[c % 2], Bu("CBB")],
                       writes=[BK[sb2]], inc=True, waw=(c != 0))
                muv = PF[:, MU_O:MU_O + 512]
                s2v = PF[:, S2_O:S2_O + 512]
                rstd = PF[:, RSTD_O:RSTD_O + 512]
                Bv = PF[:, BB_O:BB_O + 512]
                E("vector", lambda e: e.tensor_copy(muv, banks[mb][:, 0:512]),
                  reads=[BK[mb]], writes=[zmu])
                E("vector", lambda e: e.tensor_copy(s2v, banks[sb2][:, 0:512]),
                  reads=[BK[sb2]], writes=[zs2])
                E("vector", lambda e: e.tensor_mul(rstd, muv, muv),
                  reads=[zmu], writes=[zrs])
                E("vector", lambda e: e.tensor_sub(rstd, s2v, rstd),
                  reads=[zs2, zrs], writes=[zrs], waw_skip=True)
                E("scalar", lambda e: e.activation(rstd, rstd, AF.Sqrt, bias=SV[:, 40:41]),
                  reads=[zrs, Bu("SVeps")], writes=[zrs], waw_skip=True)
                E("vector", lambda e: e.reciprocal(rstd, rstd),
                  reads=[zrs], writes=[zrs], waw_skip=True)
                E("vector", lambda e: e.tensor_mul(Bv, muv, rstd),
                  reads=[zmu, zrs], writes=[zbv])
                for c in range(NCH):
                    m1 = PF[:, M1_O + (c % 2) * 512: M1_O + (c % 2) * 512 + 512]
                    if kind == "ffn":
                        xc = build_xcast(c, zlx[c % 2])
                        E("vector", lambda e, m1=m1, xc=xc: e.tensor_mul(
                            m1, xc, rstd),
                          reads=[zlx[c % 2], zrs], writes=[zm1[c % 2]])
                        E("vector", lambda e, m1=m1: e.tensor_add(m1, m1, Bv),
                          reads=[zbv, zm1[c % 2]], writes=[zm1[c % 2]],
                          waw_skip=True)
                        dst = PB[:, XNB_O + c * 2048 + t0:
                                 XNB_O + c * 2048 + t0 + 512]
                        zxn = em.zone("b", XNB_O + c * 2048 + t0, 512,
                                      f"xnb_{c}_{tau}")
                        E("scalar", lambda e, dst=dst, m1=m1, c=c: e.activation(
                            dst, m1, AF.Identity, bias=smc(f"bl{li}", c),
                            scale=smc(f"g{li}", c)),
                          reads=[zm1[c % 2], Bu("SM")], writes=[zxn])
                    elif kind == "final":
                        E("vector", lambda e, m1=m1, c=c: e.tensor_add(
                            m1, X[:, c * T + t0: c * T + t0 + 512],
                            PF[:, USL_O + c * 512: USL_O + (c + 1) * 512]),
                          reads=[xbuf(c, tau),
                                 em.zone("f", USL_O + c * 512, 512, f"usl{c}")],
                          writes=[zm1[c % 2]])
                        E("vector", lambda e, m1=m1: e.tensor_mul(m1, m1, rstd),
                          reads=[zrs, zm1[c % 2]], writes=[zm1[c % 2]],
                          waw_skip=True)
                        E("vector", lambda e, m1=m1, c=c: e.tensor_add(
                            X[:, c * T + t0: c * T + t0 + 512], m1, Bv),
                          reads=[zm1[c % 2], zbv], writes=[xbuf(c, tau)])
                    else:
                        E("vector", lambda e, m1=m1, c=c: e.tensor_mul(
                            m1, X[:, c * T + t0: c * T + t0 + 512], rstd),
                          reads=[xbuf(c, tau), zrs], writes=[zm1[c % 2]])
                        E("vector", lambda e, m1=m1, c=c: e.tensor_add(
                            X[:, c * T + t0: c * T + t0 + 512], m1, Bv),
                          reads=[zm1[c % 2], zbv], writes=[xbuf(c, tau)])
                if store_u:
                    for c in range(NCH):
                        dma("sync", u_dram[c * 128:(c + 1) * 128, t0:t0 + 512],
                            X[:, c * T + t0: c * T + t0 + 512],
                            reads=[xbuf(c, tau)], writes=[Bu(f"ud_{c}_{tau}")])

        def ffn_main(li):
            E = em.op
            Bu = em.B
            wfn = f"wf_{li}"
            zgb = [em.zone("b", GB_O + s * 256, 256, f"gb{s}") for s in range(2)]
            zub = [em.zone("b", UB_O + s * 256, 256, f"ub{s}") for s in range(2)]
            zxn = {(c, tt): em.zone("b", XNB_O + c * 2048 + tt * TAU, 512,
                                    f"xnb_{c}_{tt}")
                   for c in range(NCH) for tt in range(NTAU)}
            mu0 = 0
            for gi, gsz in enumerate(FFG):
                ws = WSA_O if gi % 2 == 0 else WSB_O
                zwf = [em.zone("b", ws + p * 3072, 3072, f"wf{gi % 2}_{p}")
                       for p in range(gsz)]
                for p in range(gsz):
                    mu = mu0 + p
                    dma("gpsimd", PB[:, ws + p * 3072: ws + (p + 1) * 3072],
                        dr[wfn][:, mu * 3072:(mu + 1) * 3072], writes=[zwf[p]])
                for tf in range(NTFF):
                    tf0 = tf * TFF
                    tt = tf0 // TAU
                    pend = None
                    for p in range(gsz):
                        mu = mu0 + p
                        pa = nbank(avoid=(4, 5, 6, 7))
                        pg = nbank(avoid=(4, 5, 6, 7))
                        for half, bkx in ((0, pa), (1, pg)):
                            for Dc in range(NCH):
                                MM(banks[bkx][:, 0:256],
                                   PB[:, ws + p * 3072 + half * 1024 + Dc * 128:
                                      ws + p * 3072 + half * 1024 + (Dc + 1) * 128],
                                   PB[:, XNB_O + Dc * 2048 + tf0:
                                      XNB_O + Dc * 2048 + tf0 + 256],
                                   start=(Dc == 0),
                                   reads=[zwf[p], zxn[(Dc, tt)]],
                                   writes=[BK[bkx]], inc=(Dc == 7),
                                   waw=(Dc != 0))
                        if pend is not None:
                            _ffn_w2(li, gi, *pend)
                        gb_ = PB[:, GB_O + (p % 2) * 256: GB_O + (p % 2) * 256 + 256]
                        E("scalar", lambda e, gb_=gb_, pg=pg, mu=mu: e.activation(
                            gb_, banks[pg][:, 0:256], AF.Gelu,
                            bias=smc(f"b1g{li}", mu)),
                          reads=[BK[pg], Bu("SM")], writes=[zgb[p % 2]])
                        ub_ = PB[:, UB_O + (p % 2) * 256: UB_O + (p % 2) * 256 + 256]
                        E("vector", lambda e, ub_=ub_, pa=pa, gb_=gb_, mu=mu:
                          e.scalar_tensor_tensor(ub_, banks[pa][:, 0:256],
                                                 smc(f"b1a{li}", mu), gb_,
                                                 ALU.add, ALU.mult),
                          reads=[BK[pa], Bu("SM"), zgb[p % 2]],
                          writes=[zub[p % 2]])
                        pend = (p, gsz, ws, zwf, zub[p % 2],
                                PB[:, UB_O + (p % 2) * 256: UB_O + (p % 2) * 256 + 256])
                    _ffn_w2(li, gi, *pend)
                    for fo in range(NCH):
                        yb = 4 + fo // 2
                        yo = (fo % 2) * 256
                        if gi == len(FFG) - 1:
                            E("vector", lambda e, yb=yb, yo=yo, fo=fo, tf0=tf0:
                              e.scalar_tensor_tensor(
                                  X[:, fo * T + tf0: fo * T + tf0 + 256],
                                  banks[yb][:, yo:yo + 256], smc(f"b2{li}", fo),
                                  X[:, fo * T + tf0: fo * T + tf0 + 256],
                                  ALU.add, ALU.add),
                              reads=[BK[yb], Bu("SM"), xbuf(fo, tf0 // TAU)],
                              writes=[xbuf(fo, tf0 // TAU)])
                        else:
                            E("vector", lambda e, yb=yb, yo=yo, fo=fo, tf0=tf0:
                              e.tensor_add(
                                  X[:, fo * T + tf0: fo * T + tf0 + 256],
                                  banks[yb][:, yo:yo + 256],
                                  X[:, fo * T + tf0: fo * T + tf0 + 256]),
                              reads=[BK[yb], xbuf(fo, tf0 // TAU)],
                              writes=[xbuf(fo, tf0 // TAU)])
                mu0 += gsz

        def _ffn_w2(li, gi, p, gsz, ws, zwf, zubp, ub_ap):
            for fo in range(NCH):
                yb = 4 + fo // 2
                yo = (fo % 2) * 256
                MM(banks[yb][:, yo:yo + 256],
                   PB[:, ws + p * 3072 + 2048 + fo * 128:
                      ws + p * 3072 + 2048 + (fo + 1) * 128],
                   ub_ap, start=(p == 0 and fo % 2 == 0),
                   reads=[zwf[p], zubp],
                   writes=[em.B(f"bank{yb}")],
                   inc=(fo == 7),
                   waw=not (p == 0 and fo % 2 == 0))

        def final_phase():
            E = em.op
            Bu = em.B
            for tau in range(NTAU):
                t0 = tau * TAU
                for c in range(NCH):
                    zu = em.zone("f", USL_O + c * 512, 512, f"usl{c}")
                    dma("sync", PF[:, USL_O + c * 512: USL_O + (c + 1) * 512],
                        u_dram[c * 128:(c + 1) * 128, t0:t0 + 512],
                        reads=[Bu(f"ud_{c}_{tau}")], writes=[zu])
                layernorm_onetau(tau)
                for c in range(NCH):
                    dma("sync", y_out[c * 128:(c + 1) * 128, t0:t0 + 512],
                        X[:, c * T + t0: c * T + t0 + 512],
                        reads=[xbuf(c, tau)], writes=[Bu(f"yo_{c}_{tau}")])

        def layernorm_onetau(tau):
            _ln_final_tau(tau)

        def _ln_final_tau(tau):
            E = em.op
            Bu = em.B
            t0 = tau * TAU
            mb, sb2 = 6, 7
            zlx = [em.zone("b", LXS_O + s * 512, 512, f"lxs{s}") for s in range(2)]
            zlq = [em.zone("b", LSQ_O + s * 512, 512, f"lsq{s}") for s in range(2)]
            zmu = em.zone("f", FMU_O, 512, "fmuv")
            zs2 = em.zone("f", FS2_O, 512, "fs2v")
            zfm = [em.zone("f", FM1_O + s * 512, 512, f"fm1{s}") for s in range(2)]
            muv = PF[:, FMU_O:FMU_O + 512]
            s2v = PF[:, FS2_O:FS2_O + 512]

            for c in range(NCH):
                xc = PB[:, LXS_O + (c % 2) * 512: LXS_O + (c % 2) * 512 + 512]
                m1 = PF[:, FM1_O + (c % 2) * 512: FM1_O + (c % 2) * 512 + 512]
                zu = em.zone("f", USL_O + c * 512, 512, f"usl{c}")
                E("vector", lambda e, m1=m1, c=c: e.tensor_add(
                    m1, X[:, c * T + t0: c * T + t0 + 512],
                    PF[:, USL_O + c * 512: USL_O + (c + 1) * 512]),
                  reads=[xbuf(c, tau), zu], writes=[zfm[c % 2]])
                E("scalar", lambda e, xc=xc, m1=m1: e.copy(xc, m1),
                  reads=[zfm[c % 2]], writes=[zlx[c % 2]])
                MM(banks[mb][:, 0:512], cb("onesneg", 128), xc,
                   start=(c == 0), reads=[zlx[c % 2], Bu("CBB")],
                   writes=[BK[mb]], inc=True, waw=(c != 0))
                sq = PB[:, LSQ_O + (c % 2) * 512: LSQ_O + (c % 2) * 512 + 512]
                E("vector", lambda e, sq=sq, xc=xc: e.tensor_mul(sq, xc, xc),
                  reads=[zlx[c % 2]], writes=[zlq[c % 2]])
                MM(banks[sb2][:, 0:512], cb("onespos", 128), sq,
                   start=(c == 0), reads=[zlq[c % 2], Bu("CBB")],
                   writes=[BK[sb2]], inc=True, waw=(c != 0))
            E("vector", lambda e: e.tensor_copy(muv, banks[mb][:, 0:512]),
              reads=[BK[mb]], writes=[zmu])
            E("vector", lambda e: e.tensor_copy(s2v, banks[sb2][:, 0:512]),
              reads=[BK[sb2]], writes=[zs2])
            tmp = PF[:, FM1_O:FM1_O + 512]
            E("vector", lambda e: e.tensor_mul(tmp, muv, muv),
              reads=[zmu], writes=[zfm[0]])
            E("vector", lambda e: e.tensor_sub(s2v, s2v, tmp),
              reads=[zfm[0], zs2], writes=[zs2], waw_skip=True)
            E("scalar", lambda e: e.activation(s2v, s2v, AF.Sqrt, bias=SV[:, 40:41]),
              reads=[zs2, Bu("SVeps")], writes=[zs2], waw_skip=True)
            E("vector", lambda e: e.reciprocal(s2v, s2v),
              reads=[zs2], writes=[zs2], waw_skip=True)
            E("vector", lambda e: e.tensor_mul(muv, muv, s2v),
              reads=[zs2, zmu], writes=[zmu], waw_skip=True)
            for c in range(NCH):
                m1 = PF[:, FM1_O + (c % 2) * 512: FM1_O + (c % 2) * 512 + 512]
                zu = em.zone("f", USL_O + c * 512, 512, f"usl{c}")
                E("vector", lambda e, m1=m1, c=c: e.tensor_add(
                    m1, X[:, c * T + t0: c * T + t0 + 512],
                    PF[:, USL_O + c * 512: USL_O + (c + 1) * 512]),
                  reads=[xbuf(c, tau), zu], writes=[zfm[c % 2]])
                E("vector", lambda e, m1=m1: e.tensor_mul(m1, m1, s2v),
                  reads=[zs2, zfm[c % 2]], writes=[zfm[c % 2]], waw_skip=True)
                E("vector", lambda e, m1=m1, c=c: e.tensor_add(
                    X[:, c * T + t0: c * T + t0 + 512], m1, muv),
                  reads=[zfm[c % 2], zmu], writes=[xbuf(c, tau)])

        if do_outer:
            if not cfg.get("skip_attnphase"):
                attn_phase(None, outer=True)
            if not cfg.get("skip_ln"):
                layernorm("plain", store_u=do_final)
        for li in range(n_inner):
            if do_iattn:
                attn_phase(li, outer=False)
            if do_ffn:
                layernorm("ffn", li=li)
                ffn_main(li)
        if do_final:
            final_phase()
        else:
            for tau in range(NTAU):
                t0 = tau * TAU
                for c in range(NCH):
                    dma("sync", y_out[c * 128:(c + 1) * 128, t0:t0 + 512],
                        X[:, c * T + t0: c * T + t0 + 512],
                        reads=[xbuf(c, tau)], writes=[em.B(f"yo_{c}_{tau}")])
        if cur == "sync":
            for nm in em.dmasems["sync"]:
                if em.emitted[nm] > 0:
                    engobj.wait_ge(em.sems[nm], em.emitted[nm])

    with nc.Block() as blk:
        blk.sync(lambda e: program("sync", e))
        blk.gpsimd(lambda e: program("gpsimd", e))
        blk.tensor(lambda e: program("tensor", e))
        blk.vector(lambda e: program("vector", e))
        blk.scalar(lambda e: program("scalar", e))
    ctx.close()
    return nc


def kernel(**inputs):
    shared, maps = prep_shared(inputs)
    x = np.asarray(inputs["L0_attn_x"], np.float32)
    nc = build_program(maps)
    in_maps = []
    for b in range(B):
        m = dict(shared)
        m["x0"] = np.ascontiguousarray(x[b].T)
        in_maps.append(m)
    res = run_bass_kernel_spmd(nc, in_maps, core_ids=list(range(B)))
    return np.stack([np.asarray(r["y"], np.float32).T for r in res.results], 0)
